# version 04
# baseline (speedup 1.0000x reference)
import contextlib
import numpy as np
import ml_dtypes
import concourse.bass as bass
import concourse.mybir as mybir
from concourse.bass_utils import run_bass_kernel_spmd

F32 = mybir.dt.float32
BF16 = mybir.dt.bfloat16
I32 = mybir.dt.int32
AF = mybir.ActivationFunctionType
ALU = mybir.AluOpType

P = 128
S = 1024
ST = S // 128
ND = 16
NCORES = 8
EPS = 1e-5
ENGS = ("sync", "gpsimd", "tensor", "vector", "scalar")
DEBUG_SERIAL = False
DEBUG_SERIAL_GATE = False
DEBUG_SERIAL_DOWN = False

C_LDREP, C_KDEC, C_CD, C_CDPOW, C_COEF = 0, 16, 32, 64, 192
C_NORMW, C_BQ, C_BK, C_BV, C_BO, C_SD = 320, 336, 352, 368, 384, 400
C_SG, C_SU, C_KFAC, C_MASK, C_TMPA, C_TMPB = 416, 460, 504, 512, 520, 536
CST_W = 552
R_MEAN, R_RSTD, R_TMP, R_DEN = 0, 1024, 2048, 2048


class Sem:
    __slots__ = ("h", "n")


class E:

    def __init__(self):
        self.nc = bass.Bass("TRN2", target_bir_lowering=False)
        self.q = {e: [] for e in ENGS}
        self.stack = contextlib.ExitStack()
        self.pe = self.sem("s_pe")
        self.dv = self.sem("s_dv")
        self.ac = self.sem("s_ac")
        self.pl = self.sem("s_pl")
        self.lanes = {"sync": [self.sem(f"da{i}") for i in range(6)],
                      "gpsimd": [self.sem(f"dw{i}") for i in range(6)]}
        self.lane_i = {"sync": 0, "gpsimd": 0}
        self.t = {}

    def sem(self, name):
        s = Sem()
        s.h = self.stack.enter_context(self.nc.semaphore(name))
        s.n = 0
        return s

    def on(self, eng, fn, inc=None, k=1):
        if inc is None:
            self.q[eng].append(fn)
            return None
        inc.n += k
        tok = inc.n
        self.q[eng].append(lambda e, fn=fn, s=inc, k=k: fn(e).then_inc(s.h, k))
        return tok

    def dma(self, eng, fn):
        return self.dma_group(eng, [fn])

    def dma_pin(self, eng, fns, name):
        lane = self.sem(name)
        self.lanes[eng].append(lane)
        for fn in fns:
            lane.n += 16
            self.q[eng].append(lambda e, fn=fn, s=lane: fn(e).then_inc(s.h, 16))
        return (lane, lane.n)

    def dma_group(self, eng, fns):
        lane = self.lanes[eng][self.lane_i[eng] % len(self.lanes[eng])]
        self.lane_i[eng] += 1
        for fn in fns:
            lane.n += 16
            self.q[eng].append(lambda e, fn=fn, s=lane: fn(e).then_inc(s.h, 16))
        return (lane, lane.n)

    def waitt(self, eng, tok):
        if tok is not None:
            self.wait(eng, tok[0], tok[1])

    def wait(self, eng, sem, tok):
        if tok is not None and tok > 0:
            self.q[eng].append(lambda e, s=sem, t=tok: e.wait_ge(s.h, t))

    def selfw(self, eng):
        sem = {"vector": self.dv, "scalar": self.ac, "tensor": self.pe,
               "gpsimd": self.pl}[eng]
        self.wait(eng, sem, sem.n)

    def barrier(self):
        sems = [self.pe, self.dv, self.ac, self.pl] + self.lanes["sync"] + self.lanes["gpsimd"]
        snap = [(s, s.n) for s in sems]
        for eng in ENGS:
            for s, n in snap:
                self.wait(eng, s, n)

    def finalize(self):
        for lane in self.lanes["sync"] + self.lanes["gpsimd"]:
            self.wait("sync", lane, lane.n)
        with self.nc.Block() as block:
            for eng in ENGS:
                fns = self.q[eng]
                if fns:
                    getattr(block, eng)(lambda engine, fns=fns: [f(engine) for f in fns])
        self.stack.close()
        return self.nc



def alloc_common(e, arenas=(), scan=False):
    nc = e.nc
    t = e.t
    for a in arenas:
        t[a] = nc.alloc_sbuf_tensor("ar_" + a, [P, 16 * S], BF16)
    t["cst"] = nc.alloc_sbuf_tensor("cst", [P, CST_W], F32)
    t["rowz"] = nc.alloc_sbuf_tensor("rowz", [1, 3072], F32)
    t["ldrow"] = nc.alloc_sbuf_tensor("ldrow", [1, 16], F32)
    t["cvrow"] = nc.alloc_sbuf_tensor("cvrow", [1, 16], F32)
    t["onescol"] = nc.alloc_sbuf_tensor("onescol", [P, 8], BF16)
    t["onesrowf"] = nc.alloc_sbuf_tensor("onesrowf", [1, P], F32)
    t["onesrowb"] = nc.alloc_sbuf_tensor("onesrowb", [1, P], BF16)
    t["rsrow"] = nc.alloc_sbuf_tensor("rsrow", [1, 512], BF16)
    t["ident"] = nc.alloc_sbuf_tensor("ident", [P, P], BF16)
    t["tf0"] = nc.alloc_sbuf_tensor("tf0", [P, S], F32)
    t["tf1"] = nc.alloc_sbuf_tensor("tf1", [P, S], F32)
    t["tb0"] = nc.alloc_sbuf_tensor("tb0", [P, S], BF16)
    t["tb1"] = nc.alloc_sbuf_tensor("tb1", [P, S], BF16)
    t["tb2"] = nc.alloc_sbuf_tensor("tb2", [P, S], BF16)
    t["epsr"] = nc.alloc_sbuf_tensor("epsr", [1, 1], F32)
    if scan:
        t["DT"] = nc.alloc_sbuf_tensor("DT", [P, 16 * P], BF16)
        t["qdecf"] = nc.alloc_sbuf_tensor("qdecf", [P, 16 * P], BF16)
        for n in ("ktok", "vtok", "sbf", "qd"):
            t[n] = nc.alloc_sbuf_tensor("sc_" + n, [P, P], BF16)
    t["P0"] = nc.alloc_psum_tensor("P0", [P, 1024], F32)
    t["P1"] = nc.alloc_psum_tensor("P1", [P, 1024], F32)
    for i in range(2, 6):
        t[f"P{i}"] = nc.alloc_psum_tensor(f"P{i}", [P, 512], F32)
    return t


def alloc_wsm(e, n):
    for i in range(n):
        e.t[f"wsm{i}"] = e.nc.alloc_sbuf_tensor(f"wsm{i}", [P, 16 * P], BF16)



def emit_prologue(e, ld_ap=None, cv_ap=None, vec_loads=(), scan_consts=False,
                  coef=False):
    t = e.t
    for ap, col, w in vec_loads:
        e.dma("sync", lambda s, ap=ap, col=col, w=w: s.dma_start(
            out=t["cst"][:, col:col + w], in_=ap))
    tok_ld = e.dma("sync", lambda s: s.dma_start(out=t["ldrow"][:], in_=ld_ap)) if ld_ap is not None else None
    tok_cv = e.dma("sync", lambda s: s.dma_start(out=t["cvrow"][:], in_=cv_ap)) if cv_ap is not None else None

    e.on("vector", lambda v: v.memset(t["onescol"][:, 0:1], 1.0), inc=e.dv)
    e.on("vector", lambda v: v.memset(t["onescol"][:, 1:2], -128.0), inc=e.dv)
    e.on("vector", lambda v: v.memset(t["onesrowb"][:], 1.0), inc=e.dv)
    e.on("vector", lambda v: v.memset(t["epsr"][:], EPS), inc=e.dv)
    tok_onesf = e.on("vector", lambda v: v.memset(t["onesrowf"][:], 1.0), inc=e.dv)

    iota_cp1 = t["tf0"][:, 0:128]
    iota_diff = t["tf0"][:, 128:256]
    iota_127 = t["tf0"][:, 256:272]
    tok_i1 = e.on("gpsimd", lambda g: g.iota(iota_cp1, pattern=[[1, 128]], base=1,
                  channel_multiplier=0, allow_small_or_imprecise_dtypes=True), inc=e.pl)
    tok_i2 = e.on("gpsimd", lambda g: g.iota(iota_diff, pattern=[[1, 128]], base=0,
                  channel_multiplier=-1, allow_small_or_imprecise_dtypes=True), inc=e.pl)
    tok_i3 = e.on("gpsimd", lambda g: g.iota(iota_127, pattern=[[0, 16]], base=127,
                  channel_multiplier=-1, allow_small_or_imprecise_dtypes=True), inc=e.pl)

    e.wait("vector", e.pl, tok_i2)
    e.on("vector", lambda v: v.tensor_scalar(
        t["ident"][:], iota_diff, 0.0, None, ALU.is_equal), inc=e.dv)

    if ld_ap is not None:
        e.waitt("tensor", tok_ld)
        e.wait("tensor", e.dv, tok_onesf)
        tok_ml = e.on("tensor", lambda tt: tt.matmul(
            t["P2"][:, 0:16], t["onesrowf"][:], t["ldrow"][:], start=True, stop=True), inc=e.pe)
        e.wait("vector", e.pe, tok_ml)
        tok_ldrep = e.on("vector", lambda v: v.tensor_copy(
            t["cst"][:, C_LDREP:C_LDREP + 16], t["P2"][:, 0:16]), inc=e.dv)
        if cv_ap is not None:
            e.waitt("tensor", tok_cv)
            tok_mc = e.on("tensor", lambda tt: tt.matmul(
                t["P3"][:, 0:16], t["onesrowf"][:], t["cvrow"][:], start=True, stop=True), inc=e.pe)
            e.wait("vector", e.pe, tok_mc)
            tok_cvrep = e.on("vector", lambda v: v.tensor_copy(
                t["cst"][:, C_KFAC:C_KFAC + 16], t["P3"][:, 0:16]), inc=e.dv)
        ldrep = t["cst"][:, C_LDREP:C_LDREP + 16]

        e.wait("vector", e.pl, tok_i3)
        e.wait("vector", e.dv, tok_ldrep)
        tok = e.on("vector", lambda v: v.tensor_mul(
            t["cst"][:, C_TMPA:C_TMPA + 16], iota_127, ldrep), inc=e.dv)
        e.wait("scalar", e.dv, tok)
        e.on("scalar", lambda s: s.activation(
            t["cst"][:, C_KDEC:C_KDEC + 16], t["cst"][:, C_TMPA:C_TMPA + 16], AF.Exp), inc=e.ac)
        e.wait("scalar", e.dv, tok_ldrep)
        e.on("scalar", lambda s: s.activation(
            t["cst"][:, C_CD:C_CD + 16], ldrep, AF.Exp, scale=128.0), inc=e.ac)
        if scan_consts:
            for j in range(8):
                e.on("scalar", lambda s, j=j: s.activation(
                    t["cst"][:, C_CDPOW + 16 * j:C_CDPOW + 16 * j + 16], ldrep,
                    AF.Exp, scale=128.0 * j), inc=e.ac)
            last_e = None
            last_pl = None
            for h in range(16):
                sc = t["cst"][:, C_LDREP + h:C_LDREP + h + 1]
                e.wait("vector", e.pl, last_pl)
                tok_m = e.on("vector", lambda v, sc=sc: v.tensor_scalar(
                    t["tf1"][:, 0:128], iota_diff, sc, None, ALU.mult), inc=e.dv)
                e.wait("gpsimd", e.dv, tok_m)
                e.wait("gpsimd", e.ac, last_e)
                last_pl = e.on("gpsimd", lambda g: g.affine_select(
                    out=t["tf1"][:, 128:256], in_=t["tf1"][:, 0:128],
                    pattern=[[1, 128]], base=0, channel_multiplier=-1,
                    compare_op=ALU.is_ge, fill=-10000.0), inc=e.pl)
                e.wait("scalar", e.pl, last_pl)
                last_e = e.on("scalar", lambda s, h=h: s.activation(
                    t["DT"][:, 128 * h:128 * h + 128], t["tf1"][:, 128:256],
                    AF.Exp), inc=e.ac)
                e.on("scalar", lambda s, h=h, sc=sc: s.activation(
                    t["qdecf"][:, 128 * h:128 * h + 128], iota_cp1, AF.Exp, scale=sc), inc=e.ac)
        if coef:
            e.wait("vector", e.dv, tok_cvrep)
            for p in range(8):
                tok1 = e.on("vector", lambda v, p=p: v.tensor_scalar(
                    t["cst"][:, C_TMPB:C_TMPB + 16], ldrep,
                    t["cst"][:, C_KFAC + p:C_KFAC + p + 1], None, ALU.mult), inc=e.dv)
                e.wait("scalar", e.dv, tok1)
                tok2 = e.on("scalar", lambda s: s.activation(
                    t["cst"][:, C_TMPA:C_TMPA + 16], t["cst"][:, C_TMPB:C_TMPB + 16],
                    AF.Exp), inc=e.ac)
                e.wait("vector", e.ac, tok2)
                e.on("vector", lambda v, p=p: v.tensor_scalar(
                    t["cst"][:, C_COEF + 16 * p:C_COEF + 16 * p + 16],
                    t["cst"][:, C_TMPA:C_TMPA + 16],
                    t["cst"][:, C_MASK + p:C_MASK + p + 1], None, ALU.mult), inc=e.dv)
    e.barrier()



def emit_ln(e, x_ap, xn_t):
    t = e.t
    tok_x = [None, None]
    tok_mm = None
    tok_ev = [None, None]
    for j in range(ND):
        par = j % 2
        xf = t["tf0"] if par == 0 else t["tf1"]
        e.wait("sync", e.pe, tok_ev[par])
        tok_x[par] = e.dma("sync", lambda s, xf=xf, j=j: s.dma_start(
            out=xf[:], in_=x_ap[128 * j:128 * j + 128, :]))
        xbf = t["tb0"] if par == 0 else t["tb1"]
        e.waitt("vector", tok_x[par])
        e.wait("vector", e.pe, tok_ev[par])
        tok_c = e.on("vector", lambda v, xbf=xbf, xf=xf: v.tensor_copy(xbf[:], xf[:]), inc=e.dv)
        sq = t["tb2"]
        e.waitt("scalar", tok_x[par])
        e.wait("scalar", e.pe, tok_mm)
        tok_s = e.on("scalar", lambda s, sq=sq, xf=xf: s.activation(sq[:], xf[:], AF.Square), inc=e.ac)
        e.wait("tensor", e.dv, tok_c)
        e.wait("tensor", e.ac, tok_s)
        for half in range(2):
            sl = slice(512 * half, 512 * half + 512)
            ps_s = t["P2"] if half == 0 else t["P3"]
            ps_q = t["P4"] if half == 0 else t["P5"]
            e.on("tensor", lambda tt, ps=ps_s, xbf=xbf, sl=sl, j=j: tt.matmul(
                ps[0:1, 0:512], t["onescol"][:, 0:1], xbf[:, sl],
                start=(j == 0), stop=(j == ND - 1)))
            last = e.on("tensor", lambda tt, ps=ps_q, sq=sq, sl=sl, j=j: tt.matmul(
                ps[0:1, 0:512], t["onescol"][:, 0:1], sq[:, sl],
                start=(j == 0), stop=(j == ND - 1)), inc=e.pe)
        tok_mm = last
        tok_ev[par] = last
    rz = t["rowz"]
    e.wait("vector", e.pe, tok_mm)
    e.on("vector", lambda v: v.tensor_scalar(rz[:, R_MEAN:R_MEAN + 512], t["P2"][0:1, 0:512], 1.0 / 2048, None, ALU.mult), inc=e.dv)
    e.on("vector", lambda v: v.tensor_scalar(rz[:, R_MEAN + 512:R_MEAN + 1024], t["P3"][0:1, 0:512], 1.0 / 2048, None, ALU.mult), inc=e.dv)
    e.on("vector", lambda v: v.tensor_scalar(rz[:, R_RSTD:R_RSTD + 512], t["P4"][0:1, 0:512], 1.0 / 2048, None, ALU.mult), inc=e.dv)
    e.on("vector", lambda v: v.tensor_scalar(rz[:, R_RSTD + 512:R_RSTD + 1024], t["P5"][0:1, 0:512], 1.0 / 2048, None, ALU.mult), inc=e.dv)
    e.selfw("vector")
    e.on("vector", lambda v: v.tensor_mul(rz[:, R_TMP:R_TMP + 1024], rz[:, R_MEAN:R_MEAN + 1024], rz[:, R_MEAN:R_MEAN + 1024]), inc=e.dv)
    e.selfw("vector")
    tok_var = e.on("vector", lambda v: v.tensor_sub(rz[:, R_RSTD:R_RSTD + 1024], rz[:, R_RSTD:R_RSTD + 1024], rz[:, R_TMP:R_TMP + 1024]), inc=e.dv)
    e.wait("scalar", e.dv, tok_var)
    tok_sqrt = e.on("scalar", lambda s: s.activation(rz[:, R_RSTD:R_RSTD + 1024], rz[:, R_RSTD:R_RSTD + 1024], AF.Sqrt, bias=t["epsr"][:]), inc=e.ac)
    e.wait("vector", e.ac, tok_sqrt)
    tok_rstd = e.on("vector", lambda v: v.reciprocal(rz[:, R_RSTD:R_RSTD + 1024], rz[:, R_RSTD:R_RSTD + 1024]), inc=e.dv)
    e.wait("tensor", e.dv, tok_rstd)
    tok_bc = None
    for half in range(2):
        sl = slice(512 * half, 512 * half + 512)
        e.on("tensor", lambda tt, sl=sl, half=half: tt.matmul(
            t["P0"][:, sl], t["onesrowf"][:], rz[:, R_MEAN + 512 * half:R_MEAN + 512 * half + 512], start=True, stop=True))
        tok_bc = e.on("tensor", lambda tt, sl=sl, half=half: tt.matmul(
            t["P1"][:, sl], t["onesrowf"][:], rz[:, R_RSTD + 512 * half:R_RSTD + 512 * half + 512], start=True, stop=True), inc=e.pe)
    tok_n = [None, None]
    last_dv = None
    for j in range(ND):
        par = j % 2
        xf = t["tf0"] if par == 0 else t["tf1"]
        e.wait("sync", e.dv, tok_n[par])
        if j < 2:
            e.wait("sync", e.pe, tok_mm)
        tok_x2 = e.dma("sync", lambda s, xf=xf, j=j: s.dma_start(
            out=xf[:], in_=x_ap[128 * j:128 * j + 128, :]))
        e.waitt("vector", tok_x2)
        e.wait("vector", e.pe, tok_bc)
        e.on("vector", lambda v, xf=xf: v.tensor_sub(xf[:], xf[:], t["P0"][:, :]), inc=e.dv)
        e.selfw("vector")
        tok_n[par] = e.on("vector", lambda v, xf=xf, j=j: v.tensor_mul(
            xn_t[:, S * j:S * j + S], xf[:], t["P1"][:, :]), inc=e.dv)
        last_dv = tok_n[par]
    return {"xn": last_dv, "p01_free": last_dv}



def wcol_dma(e, wsm_t, w_ap, e0, n_ch, nparts=2):
    per = n_ch // nparts
    w_r = w_ap.rearrange("(c p) e -> p c e", p=128)
    dst = wsm_t[:].rearrange("p (c e) -> p c e", e=128)
    fns = []
    for i in range(nparts):
        c0, c1 = per * i, per * (i + 1)
        fns.append(lambda g, c0=c0, c1=c1, e0=e0: g.dma_start(
            out=dst[:, c0:c1, :], in_=w_r[:, c0:c1, 128 * e0:128 * e0 + 128]))
    return e.dma_group("gpsimd", fns)


def emit_proj2048(e, xn_t, w_ap, evac, first_wait, elist=None, pin=None, win=None):
    t = e.t
    elist = list(range(ND)) if elist is None else elist
    pfree = list(pin) if pin else [None, None]
    wfree = dict(win) if win else {}
    tok_mm = None
    for i, ec in enumerate(elist):
        slot = i % 2
        buf = t[f"wsm{slot}"]
        if wfree.get(slot) is not None:
            e.wait("gpsimd", *wfree[slot])
        tok_w = wcol_dma(e, buf, w_ap, ec, ND)
        ps = t["P0"] if i % 2 == 0 else t["P1"]
        e.waitt("tensor", tok_w)
        if i < 2 and first_wait is not None:
            e.wait("tensor", *first_wait)
        if pfree[i % 2] is not None:
            e.wait("tensor", *pfree[i % 2])
        for d in range(ND):
            for half in range(2):
                tok_mm = e.on("tensor", lambda tt, ps=ps, buf=buf, d=d, half=half: tt.matmul(
                    ps[:, 512 * half:512 * half + 512], buf[:, 128 * d:128 * d + 128],
                    xn_t[:, S * d + 512 * half:S * d + 512 * half + 512],
                    start=(d == 0), stop=(d == ND - 1)),
                    inc=(e.pe if (d == ND - 1 and half == 1) else None))
        wfree[slot] = (e.pe, tok_mm)
        pfree[i % 2] = evac(ec, ps, tok_mm)
    return pfree, wfree



def build_a1():
    e = E()
    nc = e.nc
    x_ap = nc.dram_tensor("x", [2048, S], F32, kind="ExternalInput").ap()
    wq_ap = nc.dram_tensor("Wq", [2048, 2048], F32, kind="ExternalInput").ap()
    wk_ap = nc.dram_tensor("Wk", [2048, 2048], F32, kind="ExternalInput").ap()
    wv_ap = nc.dram_tensor("Wv", [2048, 2048], F32, kind="ExternalInput").ap()
    wg1_ap = nc.dram_tensor("Wg1", [2048, 128], F32, kind="ExternalInput").ap()
    ld_ap = nc.dram_tensor("ld", [1, 16], F32, kind="ExternalInput").ap()
    qT_o = nc.dram_tensor("qT", [P, 16 * S], BF16, kind="ExternalOutput").ap()
    oT_o = nc.dram_tensor("oT", [P, 16 * S], BF16, kind="ExternalOutput").ap()
    g1_o = nc.dram_tensor("g1", [P, S], BF16, kind="ExternalOutput").ap()
    m_o = nc.dram_tensor("m_loc", [P, 2048], F32, kind="ExternalOutput").ap()

    t = alloc_common(e, arenas=("A", "B", "C", "D"), scan=True)
    alloc_wsm(e, 2)
    t["S32"] = nc.alloc_sbuf_tensor("S32", [P, 2048], F32)
    t["g1T"] = nc.alloc_sbuf_tensor("g1T", [P, S], BF16)

    emit_prologue(e, ld_ap=ld_ap, scan_consts=True)
    xn = t["A"]
    qT, kT, vT = t["B"], t["C"], t["D"]
    oT = t["A"]
    ln = emit_ln(e, x_ap, xn)

    def mk_silu_evac(dst):
        def evac(ec, ps, tok):
            e.wait("scalar", e.pe, tok)
            return (e.ac, e.on("scalar", lambda s, ec=ec, ps=ps: s.activation(
                dst[:, S * ec:S * ec + S], ps[:, :], AF.Silu), inc=e.ac))
        return evac

    fw = (e.dv, ln["xn"])
    pf, wf = emit_proj2048(e, xn, wq_ap, mk_silu_evac(qT), fw)
    pf, wf = emit_proj2048(e, xn, wk_ap, mk_silu_evac(kT), None, pin=pf, win=wf)
    pf, wf = emit_proj2048(e, xn, wv_ap, mk_silu_evac(vT), None, pin=pf, win=wf)

    def g1_evac(ec, ps, tok):
        e.wait("vector", e.pe, tok)
        return (e.dv, e.on("vector", lambda v, ps=ps: v.tensor_copy(t["g1T"][:], ps[:, :]), inc=e.dv))

    pf, wf = emit_proj2048(e, xn, wg1_ap, g1_evac, None, elist=[0], pin=pf, win=wf)
    tok_g1 = e.dv.n
    e.wait("gpsimd", e.dv, tok_g1)
    e.dma("gpsimd", lambda g: g.dma_start(out=g1_o, in_=t["g1T"][:]))

    tok_s0 = e.on("vector", lambda v: v.memset(t["S32"][:], 0.0), inc=e.dv)

    e.barrier()
    free = {"t1": None, "t2": None, "S": None, "O": None, "C": None}
    psum_t1 = t["P2"].bitcast(BF16)
    psum_t2 = t["P3"].bitcast(BF16)
    for c in range(ST):
        for h in range(16):
            ksl = slice(S * h + 128 * c, S * h + 128 * c + 128)
            e.wait("tensor", e.dv, free["t1"])
            tok_t1 = e.on("tensor", lambda tt, ksl=ksl: tt.transpose(
                psum_t1[:, 0:128], kT[:, ksl], t["ident"][:]), inc=e.pe)
            e.wait("tensor", e.ac, free["t2"])
            tok_t2 = e.on("tensor", lambda tt, ksl=ksl: tt.transpose(
                psum_t2[:, 0:128], vT[:, ksl], t["ident"][:]), inc=e.pe)
            e.wait("vector", e.pe, tok_t1)
            tok_ktok = e.on("vector", lambda v, h=h: v.tensor_scalar(
                t["ktok"][:], psum_t1[:, 0:128], t["cst"][:, C_KDEC + h:C_KDEC + h + 1],
                None, ALU.mult), inc=e.dv)
            free["t1"] = tok_ktok
            e.wait("scalar", e.pe, tok_t2)
            tok_vtok = e.on("scalar", lambda s: s.copy(t["vtok"][:], psum_t2[:, 0:128]), inc=e.ac)
            free["t2"] = tok_vtok
            e.wait("tensor", e.dv, free["S"])
            tok_S = e.on("tensor", lambda tt, ksl=ksl: tt.matmul(
                t["P4"][:, 0:128], kT[:, ksl], qT[:, ksl], start=True, stop=True), inc=e.pe)
            e.wait("vector", e.pe, tok_S)
            tok_P = e.on("vector", lambda v, h=h: v.tensor_tensor(
                t["tb0"][:, 0:128], t["P4"][:, 0:128], t["DT"][:, 128 * h:128 * h + 128],
                ALU.mult), inc=e.dv)
            free["S"] = tok_P
            tok_qd = e.on("vector", lambda v, ksl=ksl, h=h: v.tensor_mul(
                t["qd"][:], qT[:, ksl], t["qdecf"][:, 128 * h:128 * h + 128]), inc=e.dv)
            if c > 0:
                tok_sbf = e.on("vector", lambda v, h=h: v.tensor_copy(
                    t["sbf"][:], t["S32"][:, 128 * h:128 * h + 128]), inc=e.dv)
            e.wait("tensor", e.dv, tok_P)
            e.wait("tensor", e.ac, tok_vtok)
            e.wait("tensor", e.ac, free["O"])
            if c > 0:
                e.wait("tensor", e.dv, tok_sbf)
                e.on("tensor", lambda tt: tt.matmul(
                    t["P5"][:, 0:128], t["vtok"][:], t["tb0"][:, 0:128], start=True, stop=False))
                tok_O = e.on("tensor", lambda tt: tt.matmul(
                    t["P5"][:, 0:128], t["sbf"][:], t["qd"][:], start=False, stop=True), inc=e.pe)
            else:
                e.wait("tensor", e.dv, tok_qd)
                tok_O = e.on("tensor", lambda tt: tt.matmul(
                    t["P5"][:, 0:128], t["vtok"][:], t["tb0"][:, 0:128], start=True, stop=True), inc=e.pe)
            e.wait("tensor", e.dv, tok_ktok)
            e.wait("tensor", e.dv, free["C"])
            tok_C = e.on("tensor", lambda tt: tt.matmul(
                t["P0"][:, 0:128], t["ktok"][:], t["vtok"][:], start=True, stop=True), inc=e.pe)
            e.wait("scalar", e.pe, tok_O)
            free["O"] = e.on("scalar", lambda s, h=h, c=c: s.copy(
                oT[:, S * h + 128 * c:S * h + 128 * c + 128], t["P5"][:, 0:128]), inc=e.ac)
            e.wait("vector", e.pe, tok_C)
            if c == 0 and h == 0:
                e.wait("vector", e.dv, tok_s0)
            e.selfw("vector")
            free["C"] = e.on("vector", lambda v, h=h: v.scalar_tensor_tensor(
                t["S32"][:, 128 * h:128 * h + 128], t["S32"][:, 128 * h:128 * h + 128],
                t["cst"][:, C_CD + h:C_CD + h + 1], t["P0"][:, 0:128],
                ALU.mult, ALU.add), inc=e.dv)

    e.wait("sync", e.dv, e.dv.n)
    e.wait("sync", e.ac, e.ac.n)
    e.dma("sync", lambda s: s.dma_start(out=m_o, in_=t["S32"][:]))
    e.dma("sync", lambda s: s.dma_start(out=qT_o, in_=qT[:]))
    e.dma("sync", lambda s: s.dma_start(out=oT_o, in_=oT[:]))
    return e.finalize()


def build_a2():
    e = E()
    nc = e.nc
    x_ap = nc.dram_tensor("x", [2048, S], F32, kind="ExternalInput").ap()
    qT_i = nc.dram_tensor("qT", [P, 16 * S], BF16, kind="ExternalInput").ap()
    oT_i = nc.dram_tensor("oT", [P, 16 * S], BF16, kind="ExternalInput").ap()
    g1_i = nc.dram_tensor("g1", [P, S], BF16, kind="ExternalInput").ap()
    m_i = nc.dram_tensor("m_all", [NCORES * P, 2048], F32, kind="ExternalInput").ap()
    wg2_ap = nc.dram_tensor("Wg2", [128, 2048], F32, kind="ExternalInput").ap()
    wo_ap = nc.dram_tensor("Wo", [2048, 2048], F32, kind="ExternalInput").ap()
    nw_ap = nc.dram_tensor("normw", [P, 16], F32, kind="ExternalInput").ap()
    ld_ap = nc.dram_tensor("ld", [1, 16], F32, kind="ExternalInput").ap()
    cv_ap = nc.dram_tensor("cv", [1, 16], F32, kind="ExternalInput").ap()
    x_out = nc.dram_tensor("x_out", [2048, S], F32, kind="ExternalOutput").ap()

    t = alloc_common(e, arenas=("A", "B"), scan=True)
    alloc_wsm(e, 2)
    t["S32"] = nc.alloc_sbuf_tensor("S32", [P, 2048], F32)
    t["g1T"] = nc.alloc_sbuf_tensor("g1T", [P, S], BF16)
    t["sstart"] = nc.alloc_sbuf_tensor("sstart", [P, 2048], BF16)
    t["wg2"] = nc.alloc_sbuf_tensor("wg2", [P, 2048], BF16)
    qT, oT = t["B"], t["A"]

    emit_prologue(e, ld_ap=ld_ap, cv_ap=cv_ap,
                  vec_loads=[(nw_ap, C_NORMW, 16)], scan_consts=True, coef=True)
    tok_q = e.dma_pin("sync", [lambda s: s.dma_start(out=qT[:], in_=qT_i)], "p_q")
    tok_o = e.dma_pin("sync", [lambda s: s.dma_start(out=oT[:], in_=oT_i)], "p_o")
    tok_g1 = e.dma_pin("sync", [lambda s: s.dma_start(out=t["g1T"][:], in_=g1_i)], "p_g1")
    tok_wg2 = e.dma_pin("gpsimd", [lambda g: g.dma_start(out=t["wg2"][:], in_=wg2_ap)], "p_wg2")

    tok_s0 = e.on("vector", lambda v: v.memset(t["S32"][:], 0.0), inc=e.dv)
    prev_dv = tok_s0
    for pcore in range(NCORES):
        e.wait("sync", e.dv, prev_dv)
        tok_m1 = e.dma_group("sync", [
            lambda s, pcore=pcore: s.dma_start(
                out=t["tf0"][:], in_=m_i[P * pcore:P * pcore + P, 0:1024]),
            lambda s, pcore=pcore: s.dma_start(
                out=t["tf1"][:], in_=m_i[P * pcore:P * pcore + P, 1024:2048])])
        e.waitt("vector", tok_m1)
        e.selfw("vector")
        for h in range(16):
            src = t["tf0"][:, 128 * h:128 * h + 128] if h < 8 else t["tf1"][:, 128 * (h - 8):128 * (h - 8) + 128]
            prev_dv = e.on("vector", lambda v, src=src, h=h, pcore=pcore: v.scalar_tensor_tensor(
                t["S32"][:, 128 * h:128 * h + 128], src,
                t["cst"][:, C_COEF + 16 * pcore + h:C_COEF + 16 * pcore + h + 1],
                t["S32"][:, 128 * h:128 * h + 128], ALU.mult, ALU.add), inc=e.dv)
    e.selfw("vector")
    tok_sb = e.on("vector", lambda v: v.tensor_copy(t["sstart"][:], t["S32"][:]), inc=e.dv)

    e.waitt("vector", tok_q)
    e.waitt("vector", tok_o)
    free_ps = [None, None]
    for h in range(16):
        for c in range(ST):
            par = c % 2
            ps = t["P2"] if par == 0 else t["P3"]
            qsl = slice(S * h + 128 * c, S * h + 128 * c + 128)
            qd_dst = t["tb0"][:, 0:128] if par == 0 else t["tb0"][:, 128:256]
            e.wait("vector", e.dv, free_ps[par])
            tok_qd = e.on("vector", lambda v, qsl=qsl, h=h, qd_dst=qd_dst: v.tensor_mul(
                qd_dst, qT[:, qsl], t["qdecf"][:, 128 * h:128 * h + 128]), inc=e.dv)
            e.wait("tensor", e.dv, tok_qd)
            if h == 0 and c == 0:
                e.wait("tensor", e.dv, tok_sb)
            tok_mm = e.on("tensor", lambda tt, ps=ps, h=h, qd_dst=qd_dst: tt.matmul(
                ps[:, 0:128], t["sstart"][:, 128 * h:128 * h + 128], qd_dst,
                start=True, stop=True), inc=e.pe)
            e.wait("vector", e.pe, tok_mm)
            free_ps[par] = e.on("vector", lambda v, ps=ps, qsl=qsl, h=h, c=c: v.scalar_tensor_tensor(
                oT[:, qsl], ps[:, 0:128],
                t["cst"][:, C_CDPOW + 16 * c + h:C_CDPOW + 16 * c + h + 1],
                oT[:, qsl], ALU.mult, ALU.add), inc=e.dv)
    tok_corr = e.dv.n

    e.waitt("tensor", tok_wg2)
    e.waitt("tensor", tok_g1)
    free_g = None
    free_r = None
    rz = t["rowz"]
    for h in range(16):
        e.wait("tensor", e.dv, free_g)
        tok_gmm = None
        for half in range(2):
            tok_gmm = e.on("tensor", lambda tt, h=h, half=half: tt.matmul(
                t["P0"][:, 512 * half:512 * half + 512], t["wg2"][:, 128 * h:128 * h + 128],
                t["g1T"][:, 512 * half:512 * half + 512], start=True, stop=True),
                inc=(e.pe if half == 1 else None))
        e.wait("scalar", e.pe, tok_gmm)
        tok_sig = e.on("scalar", lambda s: s.activation(t["tb1"][:], t["P0"][:, :], AF.Sigmoid), inc=e.ac)
        e.wait("vector", e.ac, tok_sig)
        if h == 0:
            e.wait("vector", e.dv, tok_corr)
        tok_t1 = e.on("vector", lambda v, h=h: v.tensor_mul(
            t["tb2"][:], oT[:, S * h:S * h + S], t["tb1"][:]), inc=e.dv)
        free_g = tok_t1
        e.wait("scalar", e.dv, tok_t1)
        tok_sq = e.on("scalar", lambda s: s.activation(t["tb0"][:], t["tb2"][:], AF.Square), inc=e.ac)
        e.wait("tensor", e.ac, tok_sq)
        e.wait("tensor", e.dv, free_r)
        e.on("tensor", lambda tt: tt.matmul(t["P4"][0:1, 0:512], t["onescol"][:, 0:1], t["tb0"][:, 0:512], start=True, stop=True))
        tok_ss = e.on("tensor", lambda tt: tt.matmul(t["P5"][0:1, 0:512], t["onescol"][:, 0:1], t["tb0"][:, 512:1024], start=True, stop=True), inc=e.pe)
        e.wait("vector", e.pe, tok_ss)
        e.on("vector", lambda v: v.tensor_scalar(rz[:, R_DEN:R_DEN + 512], t["P4"][0:1, 0:512], 1.0 / 128, None, ALU.mult))
        tok_r1 = e.on("vector", lambda v: v.tensor_scalar(rz[:, R_DEN + 512:R_DEN + 1024], t["P5"][0:1, 0:512], 1.0 / 128, None, ALU.mult), inc=e.dv)
        e.wait("scalar", e.dv, tok_r1)
        tok_r2 = e.on("scalar", lambda s: s.activation(rz[:, R_DEN:R_DEN + 1024], rz[:, R_DEN:R_DEN + 1024], AF.Sqrt, bias=t["epsr"][:]), inc=e.ac)
        e.wait("vector", e.ac, tok_r2)
        tok_r3 = e.on("vector", lambda v: v.reciprocal(rz[:, R_DEN:R_DEN + 1024], rz[:, R_DEN:R_DEN + 1024]), inc=e.dv)
        e.wait("tensor", e.dv, tok_r3)
        tok_bc = None
        for half in range(2):
            tok_bc = e.on("tensor", lambda tt, half=half: tt.matmul(
                t["P1"][:, 512 * half:512 * half + 512], t["onesrowf"][:],
                rz[:, R_DEN + 512 * half:R_DEN + 512 * half + 512], start=True, stop=True),
                inc=(e.pe if half == 1 else None))
        e.wait("vector", e.pe, tok_bc)
        free_r = e.on("vector", lambda v, h=h: v.scalar_tensor_tensor(
            oT[:, S * h:S * h + S], t["P1"][:, :],
            t["cst"][:, C_NORMW + h:C_NORMW + h + 1], t["tb2"][:],
            ALU.mult, ALU.mult), inc=e.dv)
    tok_rms = e.dv.n
    e.barrier()

    store_tok = [None, None]

    def evac_res(ec, ps, tok):
        par = ec % 2
        xf = t["tf0"] if par == 0 else t["tf1"]
        e.waitt("sync", store_tok[par])
        tok_r = e.dma("sync", lambda s, ec=ec, xf=xf: s.dma_start(
            out=xf[:], in_=x_ap[128 * ec:128 * ec + 128, :]))
        e.waitt("vector", tok_r)
        e.wait("vector", e.pe, tok)
        tok_add = e.on("vector", lambda v, ps=ps, xf=xf: v.tensor_add(xf[:], ps[:, :], xf[:]), inc=e.dv)
        e.wait("sync", e.dv, tok_add)
        store_tok[par] = e.dma("sync", lambda s, ec=ec, xf=xf: s.dma_start(
            out=x_out[128 * ec:128 * ec + 128, :], in_=xf[:]))
        return (e.dv, tok_add)

    emit_proj2048(e, oT, wo_ap, evac_res, (e.dv, tok_rms))
    return e.finalize()


def build_ffn0():
    e = E()
    nc = e.nc
    x_ap = nc.dram_tensor("x", [2048, S], F32, kind="ExternalInput").ap()
    wg_ap = nc.dram_tensor("wgT", [2048, 8192], F32, kind="ExternalInput").ap()
    wu_ap = nc.dram_tensor("wuT", [2048, 8192], F32, kind="ExternalInput").ap()
    wd_ap = nc.dram_tensor("wdT", [8192, 2048], F32, kind="ExternalInput").ap()
    x_out = nc.dram_tensor("x_out", [2048, S], F32, kind="ExternalOutput").ap()

    t = alloc_common(e, arenas=("A", "B", "C", "D"))
    alloc_wsm(e, 4)
    xn = t["A"]
    emit_prologue(e)
    ln = emit_ln(e, x_ap, xn)

    NH = 64

    def gsl(h):
        return (t["B"] if h < 32 else t["C"])[:, 512 * (h % 32):512 * (h % 32) + 512]

    for s0 in (0, 512):
        if s0:
            e.barrier()
        wfree = {}
        psfree = {}
        tok_gated = {}
        for h in range(NH):
            tok_w = {}
            for wi, w_ap_ in ((0, wg_ap), (1, wu_ap)):
                slot = 2 * (h % 2) + wi
                if wfree.get(slot) is not None:
                    e.wait("gpsimd", *wfree[slot])
                tok_w[wi] = wcol_dma(e, t[f"wsm{slot}"], w_ap_, h, ND)
            psg = t["P2"] if h % 2 == 0 else t["P3"]
            psu = t["P4"] if h % 2 == 0 else t["P5"]
            e.waitt("tensor", tok_w[0])
            e.waitt("tensor", tok_w[1])
            if h < 2 and s0 == 0:
                e.wait("tensor", e.dv, ln["xn"])
            toks = {}
            for ps, wi in ((psg, 0), (psu, 1)):
                if psfree.get((h % 2, wi)) is not None:
                    e.wait("tensor", *psfree[(h % 2, wi)])
                for d in range(ND):
                    toks[wi] = e.on("tensor", lambda tt, ps=ps, h=h, wi=wi, d=d, s0=s0: tt.matmul(
                        ps[:, 0:512], t[f"wsm{2 * (h % 2) + wi}"][:, 128 * d:128 * d + 128],
                        xn[:, S * d + s0:S * d + s0 + 512],
                        start=(d == 0), stop=(d == ND - 1)),
                        inc=(e.pe if d == ND - 1 else None))
            wfree[2 * (h % 2) + 0] = (e.pe, toks[0])
            wfree[2 * (h % 2) + 1] = (e.pe, toks[1])
            sg = t["tb0"][:, 0:512] if h % 2 == 0 else t["tb1"][:, 0:512]
            e.wait("scalar", e.pe, toks[0])
            e.wait("scalar", e.dv, tok_gated.get(h - 2))
            tok_sg = e.on("scalar", lambda s, sg=sg, psg=psg: s.activation(sg, psg[:, 0:512], AF.Silu), inc=e.ac)
            psfree[(h % 2, 0)] = (e.ac, tok_sg)
            e.wait("vector", e.ac, tok_sg)
            e.wait("vector", e.pe, toks[1])
            tok_gated[h] = e.on("vector", lambda v, h=h, sg=sg, psu=psu: v.tensor_tensor(
                gsl(h), sg, psu[:, 0:512], ALU.mult), inc=e.dv)
            psfree[(h % 2, 1)] = (e.dv, tok_gated[h])
        tok_gate_all = e.dv.n

        wbig = [t["D"][:, 0:8192], t["D"][:, 8192:16384]]
        wb_free = [None, None]
        ps_free = [None, None]
        store_tok = [None, None]
        for ec in range(ND):
            par = ec % 2
            e.wait("gpsimd", e.pe, wb_free[par])
            wdst = wbig[par].rearrange("p (c e) -> p c e", e=128)
            w_r = wd_ap.rearrange("(c p) e -> p c e", p=128)
            tok_wd = e.dma_group("gpsimd", [
                (lambda g, i=i, ec=ec, wdst=wdst, w_r=w_r: g.dma_start(
                    out=wdst[:, 8 * i:8 * i + 8, :],
                    in_=w_r[:, 8 * i:8 * i + 8, 128 * ec:128 * ec + 128]))
                for i in range(8)])
            ps = t["P0"] if par == 0 else t["P1"]
            e.waitt("tensor", tok_wd)
            if ec < 2:
                e.wait("tensor", e.dv, tok_gate_all)
            if ps_free[par] is not None:
                e.wait("tensor", *ps_free[par])
            tok_mm = None
            for h in range(NH):
                tok_mm = e.on("tensor", lambda tt, ps=ps, par=par, h=h: tt.matmul(
                    ps[:, 0:512], wbig[par][:, 128 * h:128 * h + 128], gsl(h),
                    start=(h == 0), stop=(h == NH - 1)),
                    inc=(e.pe if h == NH - 1 else None))
            wb_free[par] = tok_mm
            xf = t["tf0"] if par == 0 else t["tf1"]
            if ec < 2:
                e.wait("sync", e.dv, tok_gate_all)
            e.waitt("sync", store_tok[par])
            tok_r = e.dma("sync", lambda s, ec=ec, xf=xf, s0=s0: s.dma_start(
                out=xf[:, 0:512], in_=x_ap[128 * ec:128 * ec + 128, s0:s0 + 512]))
            e.waitt("vector", tok_r)
            e.wait("vector", e.pe, tok_mm)
            tok_add = e.on("vector", lambda v, ps=ps, xf=xf: v.tensor_add(
                xf[:, 0:512], ps[:, 0:512], xf[:, 0:512]), inc=e.dv)
            ps_free[par] = (e.dv, tok_add)
            e.wait("sync", e.dv, tok_add)
            store_tok[par] = e.dma("sync", lambda s, ec=ec, xf=xf, s0=s0: s.dma_start(
                out=x_out[128 * ec:128 * ec + 128, s0:s0 + 512], in_=xf[:, 0:512]))
    return e.finalize()


def build_c1():
    e = E()
    nc = e.nc
    x_ap = nc.dram_tensor("x", [2048, S], F32, kind="ExternalInput").ap()
    wq_ap = nc.dram_tensor("Wq", [2048, 2048], F32, kind="ExternalInput").ap()
    wk_ap = nc.dram_tensor("Wk", [2048, 2048], F32, kind="ExternalInput").ap()
    wv_ap = nc.dram_tensor("Wv", [2048, 2048], F32, kind="ExternalInput").ap()
    bq_ap = nc.dram_tensor("bq", [P, 16], F32, kind="ExternalInput").ap()
    bk_ap = nc.dram_tensor("bk", [P, 16], F32, kind="ExternalInput").ap()
    bv_ap = nc.dram_tensor("bv", [P, 16], F32, kind="ExternalInput").ap()
    q_o = nc.dram_tensor("q_loc", [P, 16 * S], BF16, kind="ExternalOutput").ap()
    k_o = nc.dram_tensor("k_loc", [2048, S], BF16, kind="ExternalOutput").ap()
    v_o = nc.dram_tensor("v_loc", [S, 2048], BF16, kind="ExternalOutput").ap()

    t = alloc_common(e, arenas=("A", "B", "C", "D"))
    alloc_wsm(e, 2)
    xn, qT, kT, vT = t["A"], t["B"], t["C"], t["D"]
    vtok = t["A"]
    emit_prologue(e, vec_loads=[(bq_ap, C_BQ, 16), (bk_ap, C_BK, 16), (bv_ap, C_BV, 16)])
    ln = emit_ln(e, x_ap, xn)

    def mk_bias_evac(dst, bc):
        def evac(ec, ps, tok):
            e.wait("vector", e.pe, tok)
            return (e.dv, e.on("vector", lambda v, ec=ec, ps=ps: v.tensor_scalar(
                dst[:, S * ec:S * ec + S], ps[:, :],
                t["cst"][:, bc + ec:bc + ec + 1], None, ALU.add), inc=e.dv))
        return evac

    fw = (e.dv, ln["xn"])
    pf, wf = emit_proj2048(e, xn, wq_ap, mk_bias_evac(qT, C_BQ), fw)
    pf, wf = emit_proj2048(e, xn, wk_ap, mk_bias_evac(kT, C_BK), None, pin=pf, win=wf)
    pf, wf = emit_proj2048(e, xn, wv_ap, mk_bias_evac(vT, C_BV), None, pin=pf, win=wf)
    e.barrier()
    tok_proj = e.dv.n

    psum_t1 = t["P2"].bitcast(BF16)
    psum_t2 = t["P3"].bitcast(BF16)
    free_t = [None, None]
    for ec in range(ND):
        for ti in range(ST):
            par = ti % 2
            pst = psum_t1 if par == 0 else psum_t2
            e.wait("tensor", e.dv, free_t[par])
            tok_tr = e.on("tensor", lambda tt, ec=ec, ti=ti, pst=pst: tt.transpose(
                pst[:, 0:128], vT[:, S * ec + 128 * ti:S * ec + 128 * ti + 128],
                t["ident"][:]), inc=e.pe)
            e.wait("vector", e.pe, tok_tr)
            free_t[par] = e.on("vector", lambda v, ec=ec, ti=ti, pst=pst: v.tensor_copy(
                vtok[:, 2048 * ti + 128 * ec:2048 * ti + 128 * ec + 128], pst[:, 0:128]), inc=e.dv)
    tok_tr_all = e.dv.n

    e.wait("sync", e.dv, tok_proj)
    e.dma("sync", lambda s: s.dma_start(out=q_o, in_=qT[:]))
    for ec in range(ND):
        e.dma("sync", lambda s, ec=ec: s.dma_start(
            out=k_o[128 * ec:128 * ec + 128, :], in_=kT[:, S * ec:S * ec + S]))
    e.wait("sync", e.dv, tok_tr_all)
    for ti in range(ST):
        e.dma("sync", lambda s, ti=ti: s.dma_start(
            out=v_o[128 * ti:128 * ti + 128, :], in_=vtok[:, 2048 * ti:2048 * ti + 2048]))
    return e.finalize()


def build_c2():
    e = E()
    nc = e.nc
    x_ap = nc.dram_tensor("x", [2048, S], F32, kind="ExternalInput").ap()
    q_i = nc.dram_tensor("q_loc", [P, 16 * S], BF16, kind="ExternalInput").ap()
    k_i = nc.dram_tensor("kfull", [NCORES * 2048, S], BF16, kind="ExternalInput").ap()
    v_i = nc.dram_tensor("vfull", [NCORES * S, 2048], BF16, kind="ExternalInput").ap()
    wo_ap = nc.dram_tensor("Wo", [2048, 2048], F32, kind="ExternalInput").ap()
    bo_ap = nc.dram_tensor("bo", [P, 16], F32, kind="ExternalInput").ap()
    x_out = nc.dram_tensor("x_out", [2048, S], F32, kind="ExternalOutput").ap()

    t = alloc_common(e, arenas=("A", "B", "C"))
    alloc_wsm(e, 2)
    oT = t["B"]
    emit_prologue(e, vec_loads=[(bo_ap, C_BO, 16)])

    NK = 64
    kh_free = [None, None]
    tok_ld = {}

    def load_head(h):
        par = h % 2
        ka = t["A"][:, 8192 * par:8192 * par + 8192]
        va = t["C"][:, 8192 * par:8192 * par + 8192]
        qa = t["tb0"] if par == 0 else t["tb1"]
        e.wait("sync", e.dv, kh_free[par])
        fns = []
        for r in range(NCORES):
            fns.append(lambda s, r=r, h=h, ka=ka: s.dma_start(
                out=ka[:, 1024 * r:1024 * r + 1024],
                in_=k_i[2048 * r + 128 * h:2048 * r + 128 * h + 128, :]))
            vsrc = v_i[S * r:S * r + S, 128 * h:128 * h + 128].rearrange("(c p) e -> p c e", p=128)
            fns.append(lambda s, r=r, h=h, va=va, vsrc=vsrc: s.dma_start(
                out=va[:, 1024 * r:1024 * r + 1024].rearrange("p (c e) -> p c e", e=128),
                in_=vsrc))
        fns.append(lambda s, h=h, qa=qa: s.dma_start(
            out=qa[:], in_=q_i[:, S * h:S * h + S]))
        tok_ld[h] = e.dma_group("sync", fns)
        return ka, va, qa

    bufs = {0: load_head(0)}
    prev_ev = None
    rz = t["rowz"]
    for h in range(16):
        ka, va, qa = bufs[h]
        if h + 1 < 16:
            bufs[h + 1] = load_head(h + 1)
        par = h % 2
        e.waitt("tensor", tok_ld[h])
        for q5 in range(2):
            tok_exp = {}
            tok_mm1 = {}
            tok_mm3 = {}
            e.wait("tensor", e.dv, prev_ev)
            for kc in range(NK + 1):
                kpar = kc % 2
                pss = t["P2"] if kpar == 0 else t["P3"]
                pt = t["tb2"][:, 0:512] if kpar == 0 else t["tb2"][:, 512:1024]
                if kc < NK:
                    e.wait("tensor", e.ac, tok_exp.get(kc - 2))
                    tok_mm1[kc] = e.on("tensor", lambda tt, pss=pss, kc=kc, ka=ka, qa=qa, q5=q5: tt.matmul(
                        pss[:, 0:512], ka[:, 128 * kc:128 * kc + 128],
                        qa[:, 512 * q5:512 * q5 + 512], start=True, stop=True), inc=e.pe)
                    e.wait("scalar", e.pe, tok_mm1[kc])
                    e.wait("scalar", e.pe, tok_mm3.get(kc - 2))
                    if kc == 0:
                        e.wait("scalar", e.dv, prev_ev)
                    tok_exp[kc] = e.on("scalar", lambda s, pss=pss, pt=pt: s.activation(
                        pt, pss[:, 0:512], AF.Exp, scale=1.0 / 128), inc=e.ac)
                if kc > 0:
                    kcp = kc - 1
                    ptp = t["tb2"][:, 0:512] if kcp % 2 == 0 else t["tb2"][:, 512:1024]
                    e.wait("tensor", e.ac, tok_exp[kcp])
                    e.on("tensor", lambda tt, kcp=kcp, va=va, ptp=ptp: tt.matmul(
                        t["P0"][:, 0:512], va[:, 128 * kcp:128 * kcp + 128], ptp,
                        start=(kcp == 0), stop=(kcp == NK - 1)))
                    tok_mm3[kcp] = e.on("tensor", lambda tt, kcp=kcp, ptp=ptp: tt.matmul(
                        t["P4"][0:1, 0:512], t["onescol"][:, 0:1], ptp,
                        start=(kcp == 0), stop=(kcp == NK - 1)), inc=e.pe)
            e.wait("vector", e.pe, tok_mm3[NK - 1])
            e.on("vector", lambda v: v.tensor_copy(rz[:, R_DEN:R_DEN + 512], t["P4"][0:1, 0:512]), inc=e.dv)
            e.selfw("vector")
            tok_d2 = e.on("vector", lambda v: v.reciprocal(rz[:, R_DEN:R_DEN + 512], rz[:, R_DEN:R_DEN + 512]), inc=e.dv)
            e.wait("tensor", e.dv, tok_d2)
            tok_bc = e.on("tensor", lambda tt: tt.matmul(
                t["P5"][:, 0:512], t["onesrowf"][:], rz[:, R_DEN:R_DEN + 512],
                start=True, stop=True), inc=e.pe)
            e.wait("vector", e.pe, tok_bc)
            e.on("vector", lambda v: v.tensor_copy(t["tf0"][:, 0:512], t["P5"][:, 0:512]), inc=e.dv)
            e.selfw("vector")
            prev_ev = e.on("vector", lambda v, h=h, q5=q5: v.tensor_tensor(
                oT[:, S * h + 512 * q5:S * h + 512 * q5 + 512], t["P0"][:, 0:512],
                t["tf0"][:, 0:512], ALU.mult), inc=e.dv)
        kh_free[par] = prev_ev
    tok_attn = prev_ev
    e.barrier()

    store_tok = [None]

    def evac_res(ec, ps, tok):
        xf = t["tf1"]
        e.waitt("sync", store_tok[0])
        tok_r = e.dma("sync", lambda s, ec=ec: s.dma_start(
            out=xf[:], in_=x_ap[128 * ec:128 * ec + 128, :]))
        e.waitt("vector", tok_r)
        e.wait("vector", e.pe, tok)
        tok_add = e.on("vector", lambda v, ps=ps, ec=ec: v.scalar_tensor_tensor(
            xf[:], ps[:, :], t["cst"][:, C_BO + ec:C_BO + ec + 1], xf[:],
            ALU.add, ALU.add), inc=e.dv)
        e.wait("sync", e.dv, tok_add)
        store_tok[0] = e.dma("sync", lambda s, ec=ec: s.dma_start(
            out=x_out[128 * ec:128 * ec + 128, :], in_=xf[:]))
        return (e.dv, tok_add)

    emit_proj2048(e, oT, wo_ap, evac_res, (e.dv, tok_attn))
    return e.finalize()


def build_ffn1():
    e = E()
    nc = e.nc
    x_ap = nc.dram_tensor("x", [2048, S], F32, kind="ExternalInput").ap()
    wg_ap = nc.dram_tensor("wgT", [2048, 5632], I32, kind="ExternalInput").ap()
    wu_ap = nc.dram_tensor("wuT", [2048, 5632], I32, kind="ExternalInput").ap()
    wd_ap = nc.dram_tensor("wdT", [5632, 2048], I32, kind="ExternalInput").ap()
    sg_ap = nc.dram_tensor("sg", [P, 44], F32, kind="ExternalInput").ap()
    su_ap = nc.dram_tensor("su", [P, 44], F32, kind="ExternalInput").ap()
    sd_ap = nc.dram_tensor("sd", [P, 16], F32, kind="ExternalInput").ap()
    x_out = nc.dram_tensor("x_out", [2048, S], F32, kind="ExternalOutput").ap()

    t = alloc_common(e, arenas=("A", "B", "C"))
    alloc_wsm(e, 4)
    t["wd0"] = nc.alloc_sbuf_tensor("wd0", [P, 44 * P], BF16)
    t["wd1"] = nc.alloc_sbuf_tensor("wd1", [P, 44 * P], BF16)
    xn = t["A"]
    emit_prologue(e, vec_loads=[(sg_ap, C_SG, 44), (su_ap, C_SU, 44), (sd_ap, C_SD, 16)])
    ln = emit_ln(e, x_ap, xn)

    NH = 44

    def gsl(h):
        return (t["B"] if h < 32 else t["C"])[:, 512 * (h % 32):512 * (h % 32) + 512]

    for s0 in (0, 512):
        if s0:
            e.barrier()
        wfree = {}
        psfree = {}
        tok_gated = {}
        for h in range(NH):
            if DEBUG_SERIAL or DEBUG_SERIAL_GATE:
                e.barrier()
            tok_w = {}
            for wi, w_ap_ in ((0, wg_ap), (1, wu_ap)):
                slot = 2 * (h % 2) + wi
                if wfree.get(slot) is not None:
                    e.wait("gpsimd", *wfree[slot])
                tok_w[wi] = wcol_dma(e, t[f"wsm{slot}"], w_ap_, h, ND)
            psg = t["P2"] if h % 2 == 0 else t["P3"]
            psu = t["P4"] if h % 2 == 0 else t["P5"]
            if h < 2 and s0 == 0:
                e.wait("tensor", e.dv, ln["xn"])
            toks = {}
            for ps, wi in ((psg, 0), (psu, 1)):
                e.waitt("tensor", tok_w[wi])
                if psfree.get((h % 2, wi)) is not None:
                    e.wait("tensor", *psfree[(h % 2, wi)])
                for d in range(ND):
                    toks[wi] = e.on("tensor", lambda tt, ps=ps, h=h, wi=wi, d=d, s0=s0: tt.matmul(
                        ps[:, 0:512], t[f"wsm{2 * (h % 2) + wi}"][:, 128 * d:128 * d + 128],
                        xn[:, S * d + s0:S * d + s0 + 512],
                        start=(d == 0), stop=(d == ND - 1)),
                        inc=(e.pe if d == ND - 1 else None))
            wfree[2 * (h % 2) + 0] = (e.pe, toks[0])
            wfree[2 * (h % 2) + 1] = (e.pe, toks[1])
            sg = t["tb0"][:, 0:512] if h % 2 == 0 else t["tb1"][:, 0:512]
            e.wait("scalar", e.pe, toks[0])
            e.wait("scalar", e.dv, tok_gated.get(h - 2))
            tok_sg = e.on("scalar", lambda s, sg=sg, psg=psg, h=h: s.activation(
                sg, psg[:, 0:512], AF.Silu, scale=t["cst"][:, C_SG + h:C_SG + h + 1]), inc=e.ac)
            psfree[(h % 2, 0)] = (e.ac, tok_sg)
            e.wait("vector", e.ac, tok_sg)
            e.wait("vector", e.pe, toks[1])
            tok_gated[h] = e.on("vector", lambda v, h=h, sg=sg, psu=psu: v.scalar_tensor_tensor(
                gsl(h), psu[:, 0:512], t["cst"][:, C_SU + h:C_SU + h + 1], sg,
                ALU.mult, ALU.mult), inc=e.dv)
            psfree[(h % 2, 1)] = (e.dv, tok_gated[h])
        tok_gate_all = e.dv.n

        e.wait("tensor", e.dv, tok_gate_all)
        e.wait("tensor", e.ac, e.ac.n)
        tok_rs = None
        for h in range(NH):
            tok_rs = e.on("tensor", lambda tt, h=h: tt.matmul(
                t["P2"][0:1, 0:512], t["onescol"][:, 1:2], gsl(h),
                start=(h == 0), stop=(h == NH - 1)),
                inc=(e.pe if h == NH - 1 else None))
        e.wait("vector", e.pe, tok_rs)
        tok_rsrow = e.on("vector", lambda v: v.tensor_copy(
            t["rowz"][:, R_TMP:R_TMP + 512], t["P2"][0:1, 0:512]), inc=e.dv)

        wb_free = [None, None]
        ps_free = [None, None]
        store_tok = [None, None]
        for ec in range(ND):
            par = ec % 2
            if DEBUG_SERIAL or DEBUG_SERIAL_DOWN:
                e.barrier()
            wd = t["wd0"] if par == 0 else t["wd1"]
            e.wait("gpsimd", e.pe, wb_free[par])
            wdst = wd[:].rearrange("p (c e) -> p c e", e=128)
            w_r = wd_ap.rearrange("(c p) e -> p c e", p=128)
            tok_wd = e.dma_group("gpsimd", [
                (lambda g, c0=11 * i, c1=11 * (i + 1), ec=ec, wdst=wdst, w_r=w_r: g.dma_start(
                    out=wdst[:, c0:c1, :], in_=w_r[:, c0:c1, 128 * ec:128 * ec + 128]))
                for i in range(4)])
            ps = t["P0"] if par == 0 else t["P1"]
            e.waitt("tensor", tok_wd)
            if ec < 2:
                e.wait("tensor", e.dv, tok_rsrow)
            if ps_free[par] is not None:
                e.wait("tensor", *ps_free[par])
            tok_mm = None
            for h in range(NH):
                tok_mm = e.on("tensor", lambda tt, ps=ps, wd=wd, h=h: tt.matmul(
                    ps[:, 0:512], wd[:, 128 * h:128 * h + 128], gsl(h),
                    start=(h == 0), stop=False))
            tok_mm = e.on("tensor", lambda tt, ps=ps: tt.matmul(
                ps[:, 0:512], t["onesrowf"][:], t["rowz"][:, R_TMP:R_TMP + 512],
                start=False, stop=True), inc=e.pe)
            wb_free[par] = tok_mm
            xf = t["tf0"] if par == 0 else t["tf1"]
            if ec < 2:
                e.wait("sync", e.dv, tok_rsrow)
            e.waitt("sync", store_tok[par])
            tok_r = e.dma("sync", lambda s, ec=ec, xf=xf, s0=s0: s.dma_start(
                out=xf[:, 0:512], in_=x_ap[128 * ec:128 * ec + 128, s0:s0 + 512]))
            e.waitt("vector", tok_r)
            e.wait("vector", e.pe, tok_mm)
            tok_add = e.on("vector", lambda v, ps=ps, xf=xf, ec=ec: v.scalar_tensor_tensor(
                xf[:, 0:512], ps[:, 0:512], t["cst"][:, C_SD + ec:C_SD + ec + 1],
                xf[:, 0:512], ALU.mult, ALU.add), inc=e.dv)
            ps_free[par] = (e.dv, tok_add)
            e.wait("sync", e.dv, tok_add)
            store_tok[par] = e.dma("sync", lambda s, ec=ec, xf=xf, s0=s0: s.dma_start(
                out=x_out[128 * ec:128 * ec + 128, s0:s0 + 512], in_=xf[:, 0:512]))
    return e.finalize()



_cache = {}


def _get(name, fn):
    if name not in _cache:
        _cache[name] = fn()
    return _cache[name]


def _bcol(v, nch):
    return np.ascontiguousarray(np.asarray(v, np.float32).reshape(nch, 128).T)


def kernel(L0_attn_x, L0_attn_log_decay, L0_attn_Wq, L0_attn_Wk, L0_attn_Wv,
           L0_attn_Wo, L0_attn_Wg1, L0_attn_Wg2, L0_attn_norm_w,
           L0_ffn_w_gate, L0_ffn_w_up, L0_ffn_w_down,
           L1_attn_Wq, L1_attn_bq, L1_attn_Wk, L1_attn_bk, L1_attn_Wv,
           L1_attn_bv, L1_attn_Wo, L1_attn_bo,
           L1_ffn_wq_gate, L1_ffn_s_gate, L1_ffn_wq_up, L1_ffn_s_up,
           L1_ffn_wq_down, L1_ffn_s_down):
    f32 = np.float32
    xT = np.ascontiguousarray(np.asarray(L0_attn_x, f32)[0].T)
    ld = np.ascontiguousarray(np.asarray(L0_attn_log_decay, f32).reshape(1, 16))
    cores = list(range(NCORES))
    xs = [np.ascontiguousarray(xT[:, S * i:S * i + S]) for i in cores]

    nc = _get("a1", build_a1)
    im = [{"x": xs[i], "Wq": L0_attn_Wq, "Wk": L0_attn_Wk, "Wv": L0_attn_Wv,
           "Wg1": L0_attn_Wg1, "ld": ld} for i in cores]
    r1 = run_bass_kernel_spmd(nc, im, core_ids=cores).results
    m_all = np.ascontiguousarray(np.concatenate([r["m_loc"] for r in r1], axis=0))

    nc = _get("a2", build_a2)
    nwcol = _bcol(L0_attn_norm_w, 16)
    ims = []
    for i in cores:
        kfac = np.array([1024.0 * max(i - 1 - p, 0) for p in range(8)], f32)
        mask = np.array([1.0 if p < i else 0.0 for p in range(8)], f32)
        cv = np.ascontiguousarray(np.concatenate([kfac, mask]).reshape(1, 16))
        ims.append({"x": xs[i], "qT": r1[i]["qT"], "oT": r1[i]["oT"],
                    "g1": r1[i]["g1"], "m_all": m_all, "Wg2": L0_attn_Wg2,
                    "Wo": L0_attn_Wo, "normw": nwcol, "ld": ld, "cv": cv})
    r2 = run_bass_kernel_spmd(nc, ims, core_ids=cores).results
    x1 = [r["x_out"] for r in r2]

    nc = _get("b", build_ffn0)
    wgT = np.ascontiguousarray(np.asarray(L0_ffn_w_gate, f32).T)
    wuT = np.ascontiguousarray(np.asarray(L0_ffn_w_up, f32).T)
    wdT = np.ascontiguousarray(np.asarray(L0_ffn_w_down, f32).T)
    im = [{"x": x1[i], "wgT": wgT, "wuT": wuT, "wdT": wdT} for i in cores]
    r3 = run_bass_kernel_spmd(nc, im, core_ids=cores).results
    x2 = [r["x_out"] for r in r3]

    nc = _get("c1", build_c1)
    im = [{"x": x2[i], "Wq": L1_attn_Wq, "Wk": L1_attn_Wk, "Wv": L1_attn_Wv,
           "bq": _bcol(L1_attn_bq, 16), "bk": _bcol(L1_attn_bk, 16),
           "bv": _bcol(L1_attn_bv, 16)} for i in cores]
    r4 = run_bass_kernel_spmd(nc, im, core_ids=cores).results
    kfull = np.ascontiguousarray(np.concatenate([r["k_loc"] for r in r4], axis=0))
    vfull = np.ascontiguousarray(np.concatenate([r["v_loc"] for r in r4], axis=0))

    nc = _get("c2", build_c2)
    im = [{"x": x2[i], "q_loc": r4[i]["q_loc"], "kfull": kfull, "vfull": vfull,
           "Wo": L1_attn_Wo, "bo": _bcol(L1_attn_bo, 16)} for i in cores]
    r5 = run_bass_kernel_spmd(nc, im, core_ids=cores).results
    x3 = [r["x_out"] for r in r5]

    nc = _get("d", build_ffn1)
    wg1T = np.ascontiguousarray(np.asarray(L1_ffn_wq_gate, np.int32).T)
    wu1T = np.ascontiguousarray(np.asarray(L1_ffn_wq_up, np.int32).T)
    wd1T = np.ascontiguousarray(np.asarray(L1_ffn_wq_down, np.int32).T)
    im = [{"x": x3[i], "wgT": wg1T, "wuT": wu1T, "wdT": wd1T,
           "sg": _bcol(L1_ffn_s_gate, 44), "su": _bcol(L1_ffn_s_up, 44),
           "sd": _bcol(L1_ffn_s_down, 16)} for i in cores]
    r6 = run_bass_kernel_spmd(nc, im, core_ids=cores).results

    out = np.concatenate([r["x_out"] for r in r6], axis=1)
    return np.ascontiguousarray(out.T).reshape(1, 8192, 2048)
